# version 17
# baseline (speedup 1.0000x reference)
import numpy as np
import ml_dtypes
import concourse.bass as bass
import concourse.mybir as mybir
from concourse.bass_utils import run_bass_kernel_spmd

F32 = mybir.dt.float32
BF16 = mybir.dt.bfloat16
AF = mybir.ActivationFunctionType
ALU = mybir.AluOpType
BF = ml_dtypes.bfloat16

B, S, DM, H, HD = 2, 8192, 1024, 16, 64
NC = 8
SL = S // 4
NP = H // 2
NB = S // 128
NQB = SL // 128
NCH = S // 128
QW = 512
NQW = SL // QW
GRP = [3] * 21 + [1]
NGR = len(GRP)
SHIFT = 4.0
EPS = 1e-5
KVC = 257
PW = 385
RBT = NB
NSLAB = RBT // 8
KTD_PP = 10


def cumktd(rp, s):
    if s < 0:
        return rp * KTD_PP
    return rp * KTD_PP + (2 * (s + 1) if s <= 1 else s + 3)

C_SHIFT, C_SCL, C_ONE, C_ZERO, C_NHALF, C_EPS = 0, 1, 2, 3, 4, 5

ENGS = ("sp", "pe", "act", "dve", "pool")
DEBUG_DUMPS = False
DEBUG_LN1_BYPASS = False
PHASES = "all"
LAST_RES = None


class Prog:

    def __init__(self):
        self.items = {e: [] for e in ENGS}
        self.sim_incs = {}

    def op(self, eng, fn, waits=(), inc=None, flush=False, sim_incs=None):
        if sim_incs:
            self.sim_incs[(eng, len(self.items[eng]))] = list(sim_incs)
        self.items[eng].append((tuple(waits), fn, inc, flush))


def build_nc():
    nc = bass.Bass("TRN2", target_bir_lowering=False)

    d_x = nc.dram_tensor("x", [S, DM], F32, kind="ExternalInput").ap()
    d_wkv = nc.dram_tensor("wkv", [8, 128, NP * PW], F32, kind="ExternalInput").ap()
    d_bkv = nc.dram_tensor("bkv", [1, NP * PW], F32, kind="ExternalInput").ap()
    d_csk = nc.dram_tensor("csk", [128, 2 * NB * 64], F32, kind="ExternalInput").ap()
    d_cb = nc.dram_tensor("cbv", [128, 8], F32, kind="ExternalInput").ap()
    d_eyef = nc.dram_tensor("eyef", [128, 128], F32, kind="ExternalInput").ap()
    d_win = nc.dram_tensor("ffw_in", [8, 128, 64], F32, kind="ExternalInput").ap()
    d_wout = nc.dram_tensor("ffw_out", [65, DM], F32, kind="ExternalInput").ap()
    d_ffs = nc.dram_tensor("ffs", [64, 50], F32, kind="ExternalInput").ap()
    d_fbr = nc.dram_tensor("fbr", [1, DM + 64], F32, kind="ExternalInput").ap()

    d_xlt = nc.dram_tensor("xlt", [8, 128, S], BF16, kind="Internal").ap()
    d_osc = nc.dram_tensor("osc", [SL, DM], F32, kind="Internal").ap()
    d_x1sc = nc.dram_tensor("x1sc", [SL, DM], F32, kind="Internal").ap()
    d_y = nc.dram_tensor("y", [SL, DM], F32, kind="ExternalOutput").ap()

    A = nc.alloc_sbuf_tensor
    ST = A("ST", [128, 48], F32)
    MV = A("MV", [128, 8], F32)
    T1 = A("T1", [128, 1], F32)
    RS = A("RS", [128, 4], F32)
    BKV = A("BKV", [1, NP * PW], BF16)
    ONESR = A("ONESR", [1, 128], BF16)
    FFNW = A("FFNW", [128, 8 * 64], BF16)
    WOUT = A("WOUT", [65, DM], BF16)
    FFS = A("FFS", [64, 50], F32)
    BIN = A("BIN", [1, 64], BF16)
    CB = A("CB", [128, 8], F32)
    EYEF = A("EYEF", [128, 128], F32)
    KT = A("KT", [128, 2 * S], BF16)
    QT = A("QT", [128, 2 * SL], BF16)
    VAU = A("VAU", [128, 2 * NCH * 129], BF16)
    TRS = A("TRS", [128, 256], F32)
    KTOK = A("KTOK", [128, 8 * 128], BF16)
    QTOK = A("QTOK", [128, 8 * 128], BF16)
    PTA = A("PTA", [128, 4 * 1536], BF16)
    PTB = A("PTB", [128, 4 * 1024], BF16)
    OEA = A("OEA", [128, 512], F32)
    OEB = A("OEB", [128, 512], F32)
    RD = A("RD", [128, 8], F32)
    OTD = A("OTD", [128, 256], F32)
    OTD2 = A("OTD2", [128, 256], F32)
    XB = A("XB", [128, 2 * 8192], BF16)
    XW = A("XW", [128, 4 * DM], F32)
    XLTS = A("XLTS", [128, 2 * 4 * 1024], BF16)
    WST = A("WST", [128, 8 * PW], F32)
    CSB = A("CSB", [128, 2 * 1024], F32)
    XLN = A("XLN", [128, 2 * DM], BF16)
    WKVP = A("WKVP", [128, 8 * PW], BF16)
    r1base = nc.lookup_mloc(XB).addr
    _off = [r1base]

    def AO(name, shape, dtype):
        nbytes = shape[-1] * (4 if dtype == F32 else 2)
        nbytes = (nbytes + 31) // 32 * 32
        t = nc.alloc_sbuf_tensor_at(name, shape, dtype, offset=_off[0])
        _off[0] += nbytes
        return t

    OSCT = AO("OSCT", [128, 2 * DM], F32)
    XQT = AO("XQT", [128, 2 * DM], F32)
    X1 = AO("X1", [128, 2 * DM], F32)
    TMP3 = AO("TMP3", [128, DM], F32)
    X1N = AO("X1N", [128, 2 * DM], BF16)
    X1NT = AO("X1NT", [128, 4 * DM], BF16)
    TM3 = AO("TM3", [64, 1024], F32)
    G3 = AO("G3", [64, 1024], F32)
    GB3 = AO("GB3", [65, 1024], BF16)
    YT = AO("YT", [128, DM], F32)
    MVS3 = AO("MVS3", [128, 16], F32)
    ST3 = AO("ST3", [128, 12], F32)
    MV3 = AO("MV3", [128, 2], F32)
    RSS3 = AO("RSS3", [128, 16], F32)
    SALL = AO("SALL", [64, 1024], F32)
    assert _off[0] <= r1base + 78 * 1024, "phase-3 overlay exceeds R1 region"

    PX = nc.alloc_psum_tensor("PX", [128, 1536], F32)
    PY = nc.alloc_psum_tensor("PY", [128, 1536], F32)
    POA = nc.alloc_psum_tensor("POA", [128, 512], F32)
    POB = nc.alloc_psum_tensor("POB", [128, 512], F32)

    def pslot(s):
        if s < 3:
            return PX[:, s * 512:(s + 1) * 512]
        return PY[:, (s - 3) * 512:(s - 2) * 512]

    sem = {}
    for name in ("sXL sST sSTA sRS sXA sXT sXE sXWR sWL sWC "
                 "sXBL sCSL sPJ sRP sKTB sKTE sKTD sRC "
                 "sSA sSB sEA sEB sAVA sAVB sOEA sOEB sTRA sTRB sDVA sDVB sOSW "
                 "sOSR sXQL s3ST s3RS s3X1 s3A1 s3S2 s3N s3T s3TD sX1E s3S s3M s3O s3OE s3TT s3F sYW sDBG").split():
        sem[name] = nc.alloc_semaphore(name)

    P = Prog()

    wl = [0]
    wc = [0]

    def load(dst, src, eng="sp"):
        P.op(eng, lambda e, d=dst, s=src: e.dma_start(d, s), inc=("sWL", 16))
        wl[0] += 16

    load(CB[:], d_cb)
    load(EYEF[:], d_eyef)
    load(FFS[:], d_ffs)

    P.op("dve", lambda e: e.memset(ONESR[:], 1.0), inc=("sWC", 1)); wc[0] += 1

    P.op("sp", lambda e: e.dma_start(WST[0:1, 0:NP * PW], d_bkv),
         waits=[("sWC", wc[0])], inc=("sWL", 16)); wl[0] += 16
    P.op("dve", lambda e: e.tensor_copy(BKV[:], WST[0:1, 0:NP * PW]),
         waits=[("sWL", wl[0])], inc=("sWC", 1)); wc[0] += 1
    for dc in range(8):
        P.op("sp", lambda e, dc=dc: e.dma_start(WST[:, 0:64], d_win[dc]),
             waits=[("sWC", wc[0])], inc=("sWL", 16)); wl[0] += 16
        P.op("dve", lambda e, dc=dc: e.tensor_copy(FFNW[:, dc * 64:(dc + 1) * 64], WST[:, 0:64]),
             waits=[("sWL", wl[0])], inc=("sWC", 1)); wc[0] += 1
    P.op("sp", lambda e: e.dma_start(WST[0:65, 0:DM], d_wout),
         waits=[("sWC", wc[0])], inc=("sWL", 16)); wl[0] += 16
    P.op("dve", lambda e: e.tensor_copy(WOUT[:], WST[0:65, 0:DM]),
         waits=[("sWL", wl[0])], inc=("sWC", 1)); wc[0] += 1
    P.op("sp", lambda e: e.dma_start(WST[0:1, 0:DM + 64], d_fbr),
         waits=[("sWC", wc[0])], inc=("sWL", 16)); wl[0] += 16
    P.op("dve", lambda e: e.tensor_copy(BIN[:], WST[0:1, 0:64]),
         waits=[("sWL", wl[0])], inc=("sWC", 1)); wc[0] += 1
    W_ALL = wl[0]
    WC_ALL = wc[0]

    def ph1(pump_units=()):
        NT = NB
        pu = list(pump_units)
        pi = [0]

        def pump(t):
            while pi[0] < len(pu) and pu[pi[0]][0] <= t:
                pu[pi[0]][1]()
                pi[0] += 1

        def emit_load(t):
            b3 = t % 4
            w = [("sSTA", t - 3)] if t >= 4 else []
            if t == 0:
                w.append(("sWL", W_ALL))
            P.op("act", lambda e, t=t, b=b3: e.dma_start(XW[:, b * DM:(b + 1) * DM],
                                                         d_x[t * 128:(t + 1) * 128, :]),
                 waits=w, inc=("sXL", 16))

        def emit_stats(t):
            buf = t % 4
            w2 = [("sXL", 16 * (t + 1))]
            if t >= 4:
                w2.append(("sRS", t - 3))
            P.op("dve", lambda e, b=buf: e.bn_stats(ST[:, b * 6:(b + 1) * 6], XW[:, b * DM:b * DM + 512]),
                 waits=w2)
            P.op("dve", lambda e, b=buf: e.bn_stats(ST[:, 24 + b * 6:24 + (b + 1) * 6],
                                                    XW[:, b * DM + 512:b * DM + 1024]))

        def emit_aggr(t):
            buf = t % 4
            ap_in = ST[:].rearrange("p (h x) -> p h x", h=2)[:, :, 6 * buf:6 * buf + 6]
            P.op("dve", lambda e, b=buf, a=ap_in: e.bn_aggr(MV[:, 2 * b:2 * b + 2], a),
                 inc=("sST", 1), flush=True)

        def emit_rstd(t):
            buf = t % 4
            P.op("act", lambda e, b=buf: e.activation(T1[:], MV[:, 2 * b + 1:2 * b + 2], AF.Ln,
                                                      bias=CB[:, C_EPS:C_EPS + 1], scale=CB[:, C_ONE:C_ONE + 1]),
                 waits=[("sST", t + 1)])
            P.op("act", lambda e, b=buf: e.activation(RS[:, b:b + 1], T1[:], AF.Exp,
                                                      bias=CB[:, C_ZERO:C_ZERO + 1],
                                                      scale=CB[:, C_NHALF:C_NHALF + 1]),
                 inc=("sRS", 1))

        def emit_apply(t):
            buf = t % 2
            bs = t % 4
            w = [("sRS", t + 1)]
            if t >= 2:
                w.append(("sXT", 16 * (t - 1)))
            P.op("dve", lambda e, b=buf, bs=bs: e.tensor_scalar(
                XLN[:, b * DM:(b + 1) * DM], XW[:, bs * DM:(bs + 1) * DM],
                MV[:, 2 * bs:2 * bs + 1], RS[:, bs:bs + 1], ALU.subtract, ALU.mult),
                waits=w, inc=("sSTA", 1))

        def emit_trw(t):
            buf = t % 2
            g2 = (t // 4) % 2
            w = [("sSTA", t + 1)]
            if t >= 8:
                w.append(("sXWR", 16 * (t // 4 - 1)))

            def trd(e, b=buf, tt=t, g2=g2):
                dst = XLTS[:, g2 * 4096:(g2 + 1) * 4096].rearrange(
                    "p (j u) -> p j u", j=8)[:, :, (tt % 4) * 128:(tt % 4 + 1) * 128]
                return e.dma_start_transpose(dst, XLN[:, b * DM:(b + 1) * DM])
            P.op("sp", trd, waits=w, inc=("sXT", 16))
            if t % 4 == 3:
                sl4 = t // 4
                def wr(e, sl=sl4, g2=g2):
                    dst = d_xlt[:, :, sl * 512:(sl + 1) * 512].rearrange("j p t -> p j t")
                    src = XLTS[:, g2 * 4096:(g2 + 1) * 4096].rearrange("p (j u) -> p j u", j=8)
                    return e.dma_start(dst, src)
                P.op("pool", wr, waits=[("sXT", 16 * (t + 1))], inc=("sXWR", 16))

        for lt in range(3):
            emit_load(lt)
        for t in range(NT + 2):
            pump(t)
            if t < NT:
                emit_stats(t)
            if 1 <= t <= NT:
                emit_aggr(t - 1)
                emit_rstd(t - 1)
            if 2 <= t <= NT + 1:
                emit_apply(t - 2)
                emit_trw(t - 2)
            if t + 3 < NT:
                emit_load(t + 3)
        while pi[0] < len(pu):
            pu[pi[0]][1]()
            pi[0] += 1
    XWR_ALL = 16 * (NB // 4)

    GRPB = [2] * 32
    NGRB = len(GRPB)
    slab_hist = {0: [], 1: []}
    SPARE = PY[:, 1024:1536]

    def emit_a1(i):
        buf = i % 2
        qwi = i // 4
        qq7 = NP * NQW - NQW + qwi
        w = [("sOSW", 16 * 8 * (qq7 + 1))]
        if i >= 2:
            w.append(("s3X1", 16 * (i - 1)))
        P.op("sp", lambda e, i=i, b=buf: e.dma_start(OSCT[:, b * DM:(b + 1) * DM],
                                                     d_osc[i * 128:(i + 1) * 128, :]),
             waits=w, inc=("sOSR", 16))
        wx = [("s3X1", 16 * (i - 1))] if i >= 2 else []
        P.op("sp", lambda e, i=i, b=buf: e.dma_start(XQT[:, b * DM:(b + 1) * DM],
                                                     d_x[i * 128:(i + 1) * 128, :]),
             waits=wx, inc=("sXQL", 16))
        P.op("dve", lambda e, b=buf: e.bn_stats(ST3[:, 0:6], OSCT[:, b * DM:b * DM + 512]),
             waits=[("sOSR", 16 * (i + 1)), ("s3RS", 2 * i)])
        P.op("dve", lambda e, b=buf: e.bn_stats(ST3[:, 6:12], OSCT[:, b * DM + 512:b * DM + 1024]))
        P.op("dve", lambda e: (e.drain(), e.bn_aggr(MV3[:], ST3[:]))[1], inc=("s3ST", 1), flush=True)
        P.op("act", lambda e: e.activation(T1[:], MV3[:, 1:2], AF.Ln,
                                           bias=CB[:, C_EPS:C_EPS + 1], scale=CB[:, C_ONE:C_ONE + 1]),
             waits=[("s3ST", 2 * i + 1)])
        P.op("act", lambda e: e.activation(RS[:, 0:1], T1[:], AF.Exp,
                                           bias=CB[:, C_ZERO:C_ZERO + 1], scale=CB[:, C_NHALF:C_NHALF + 1]),
             inc=("s3RS", 1), flush=True)
        P.op("dve", lambda e, b=buf: e.tensor_scalar(TMP3[:], OSCT[:, b * DM:(b + 1) * DM],
                                                     MV3[:, 0:1], RS[:, 0:1], ALU.subtract, ALU.mult),
             waits=[("s3RS", 2 * i + 1), ("sXQL", 16 * (i + 1))])
        P.op("dve", lambda e, b=buf: e.tensor_add(X1[:, b * DM:(b + 1) * DM], TMP3[:],
                                                  XQT[:, b * DM:(b + 1) * DM]),
             inc=("s3A1", 1), flush=True)
        P.op("dve", lambda e, b=buf: e.bn_stats(ST3[:, 0:6], X1[:, b * DM:b * DM + 512]))
        P.op("dve", lambda e, b=buf: e.bn_stats(ST3[:, 6:12], X1[:, b * DM + 512:b * DM + 1024]))
        P.op("dve", lambda e: (e.drain(), e.bn_aggr(MV3[:], ST3[:]))[1], inc=("s3ST", 1), flush=True)
        P.op("act", lambda e: e.activation(T1[:], MV3[:, 1:2], AF.Ln,
                                           bias=CB[:, C_EPS:C_EPS + 1], scale=CB[:, C_ONE:C_ONE + 1]),
             waits=[("s3ST", 2 * i + 2)])
        P.op("act", lambda e, i=i: e.activation(RSS3[:, i:i + 1], T1[:], AF.Exp,
                                                bias=CB[:, C_ZERO:C_ZERO + 1], scale=CB[:, C_NHALF:C_NHALF + 1]),
             inc=("s3RS", 1), flush=True)
        P.op("dve", lambda e, i=i: e.tensor_copy(MVS3[:, i:i + 1], MV3[:, 0:1]),
             waits=[("s3RS", 2 * i + 2)])
        P.op("pool", lambda e, i=i, b=buf: e.dma_start(d_x1sc[i * 128:(i + 1) * 128, :],
                                                       X1[:, b * DM:(b + 1) * DM]),
             waits=[("s3A1", i + 1)], inc=("s3X1", 16))



    def rc_units(rp):
        rb0 = rp * RBT
        units = []

        def w_unit():
            src_kv = d_wkv[:, :, rp * PW:(rp + 1) * PW].rearrange("j p c -> p j c")
            P.op("sp", lambda e, s_=src_kv: e.dma_start(
                WST[:].rearrange("p (j c) -> p j c", j=8), s_),
                 waits=[("sWC", wc[0])], inc=("sWL", 16)); wl[0] += 16
            P.op("dve", lambda e: e.tensor_copy(WKVP[:], WST[:]),
                 waits=[("sWL", wl[0]), ("sPJ", rp * RBT)], inc=("sWC", 1)); wc[0] += 1
        units.append((0, w_unit))
        wc_pair = wc[0] + 1

        def slab_unit(si):
            def f(si=si):
                buf = (rp * NSLAB + si) % 2
                w = [("sXWR", 16 * min(2 * (si + 1), XWR_ALL // 16))]
                if slab_hist[buf]:
                    w.append(("sPJ", slab_hist[buf][-1]))
                src = d_xlt[:, :, si * 1024:(si + 1) * 1024].rearrange("j p t -> p j t")
                dst = XB[:, buf * 8192:(buf + 1) * 8192].rearrange("p (j t) -> p j t", j=8)
                P.op("pool", lambda e, d=dst, s=src: e.dma_start(d, s), waits=w, inc=("sXBL", 16))
                slab_hist[buf].append(rb0 + min((si + 1) * 8, RBT))
                c_src = d_csk[:, si * 512:(si + 1) * 512]
                s_src = d_csk[:, NB * 64 + si * 512: NB * 64 + (si + 1) * 512]
                wcs = [("sRP", slab_hist[buf][-2])] if len(slab_hist[buf]) >= 2 else []
                P.op("pool", lambda e, s_=c_src, b=buf: e.dma_start(CSB[:, b * 1024:b * 1024 + 512], s_),
                     waits=wcs, inc=("sCSL", 16))
                P.op("pool", lambda e, s_=s_src, b=buf: e.dma_start(CSB[:, b * 1024 + 512:b * 1024 + 1024], s_),
                     waits=[], inc=("sCSL", 16))
            return f

        def block_unit(ib):
            def f(ib=ib):
                rb = rb0 + ib
                hasq = ib < 16
                si = ib // 8
                buf = (rp * NSLAB + si) % 2
                bs = ib % 8
                w = [("sXBL", 16 * (rp * NSLAB + si + 1))]
                w.append(("sRP", rb))
                if ib == 0:
                    w.append(("sWC", wc_pair))
                slot = SPARE
                ncols = PW if hasq else KVC

                def proj(e, rp=rp, buf=buf, bs=bs, slot=slot, ncols=ncols):
                    for dc in range(8):
                        lhsT = XB[:, buf * 8192 + dc * 1024 + bs * 128: buf * 8192 + dc * 1024 + (bs + 1) * 128]
                        rhs = WKVP[:, dc * PW:dc * PW + ncols]
                        e.matmul(slot[:, 0:ncols], lhsT, rhs, start=(dc == 0), stop=False)
                    bias = BKV[:, rp * PW:rp * PW + ncols]
                    return e.matmul(slot[:, 0:ncols], ONESR[:], bias, start=False, stop=True)
                P.op("pe", proj, waits=w, inc=("sPJ", 1))

                w = [("sPJ", rb + 1), ("sCSL", 32 * (rp * NSLAB + si + 1))]
                if not (rp == 0 and si == 0):
                    w.append(("sKTD", 16 * cumktd(rp, si - 1)))
                co = buf * 1024 + bs * 64
                par = rp % 2

                def rope2(e, slot=slot, co=co, kslot=bs, hasq=hasq, blk=ib, par=par):
                    c_ap = CSB[:, co:co + 64]
                    s_ap = CSB[:, co + 512:co + 512 + 64]
                    xr = slot[:, 0:128:2]
                    xi = slot[:, 1:128:2]
                    e.tensor_mul(TRS[:, 0:64], xr, c_ap)
                    e.tensor_mul(TRS[:, 64:128], xi, s_ap)
                    e.tensor_mul(TRS[:, 128:192], xr, s_ap)
                    e.tensor_mul(TRS[:, 192:256], xi, c_ap)
                    kt = KTOK[:, kslot * 128:(kslot + 1) * 128]
                    e.tensor_sub(kt[:, 0:128:2], TRS[:, 0:64], TRS[:, 64:128])
                    e.tensor_add(kt[:, 1:128:2], TRS[:, 128:192], TRS[:, 192:256])
                    last = e.tensor_copy(VAU[:, par * NCH * 129 + blk * 129: par * NCH * 129 + (blk + 1) * 129],
                                         slot[:, 128:KVC])
                    if hasq:
                        qr = slot[:, 257:385:2]
                        qi = slot[:, 258:385:2]
                        e.tensor_mul(TRS[:, 0:64], qr, c_ap)
                        e.tensor_mul(TRS[:, 64:128], qi, s_ap)
                        e.tensor_mul(TRS[:, 128:192], qr, s_ap)
                        e.tensor_mul(TRS[:, 192:256], qi, c_ap)
                        qt = QTOK[:, kslot * 128:(kslot + 1) * 128]
                        e.tensor_sub(qt[:, 0:128:2], TRS[:, 0:64], TRS[:, 64:128])
                        last = e.tensor_add(qt[:, 1:128:2], TRS[:, 128:192], TRS[:, 192:256])
                    return last
                P.op("dve", rope2, waits=w, inc=("sRP", 1))
            return f

        def batch_unit(s):
            def f(s=s):
                par = rp % 2
                off = par * S + s * 1024
                dq = "sp"

                def trd(e, off=off):
                    dst = KT[:, off:off + 1024].rearrange("p (j t) -> p j t", j=8)
                    return e.dma_start_transpose(dst, KTOK[:, 0:1024])
                P.op(dq, trd, waits=[("sRP", rb0 + s * 8 + 8)], inc=("sKTD", 16))
                if s <= 1:
                    offq = par * SL + s * 1024

                    def trq(e, offq=offq):
                        dst = QT[:, offq:offq + 1024].rearrange("p (j t) -> p j t", j=8)
                        return e.dma_start_transpose(dst, QTOK[:, 0:1024])
                    P.op(dq, trq, waits=[("sRP", rb0 + s * 8 + 8)], inc=("sKTD", 16))
            return f

        for si in range(NSLAB):
            units.append((8 * si + 10, slab_unit(si)))
            if si >= 1:
                units.append((8 * si + 10, batch_unit(si - 1)))
            for ib in range(si * 8, si * 8 + 8):
                units.append((8 * si + 10, block_unit(ib)))
        units.append((NB + 2, batch_unit(NSLAB - 1)))

        def done_unit():
            P.op("dve", lambda e: e.engine_nop(), waits=[("sKTD", 16 * KTD_PP * (rp + 1))], inc=("sRC", 1))
        units.append((NB + 2, done_unit))
        return units

    def _avu(p, qq, glist, g, u, PT, ptstride, PO, voff, sE, sAV):
        glen = glist[g]
        cbase = sum(glist[:g])
        buf = u % 4
        waits = [(sE, u + 1)]
        if g == 0 and qq > 0:
            waits.append(("sDVA", qq))
        par = p % 2

        def av(e, glen=glen, cbase=cbase, buf=buf, g=g, voff=voff, par=par):
            last = None
            for j in range(glen):
                c = cbase + j
                last = e.matmul(PO[0:65, 0:QW],
                                VAU[:, par * NCH * 129 + c * 129 + voff: par * NCH * 129 + c * 129 + voff + 65],
                                PT[:, buf * ptstride + j * 512: buf * ptstride + (j + 1) * 512],
                                start=(g == 0 and j == 0), stop=(g == len(glist) - 1 and j == glen - 1))
            return last
        P.op("pe", av, waits=waits, inc=(sAV, 1))

    pending_epi = []
    pending_a1 = []

    def emit_pair(p):
        par = p % 2
        units = rc_units(p + 1) if p + 1 < NP else []
        uidx = [0]
        pump_span = 3 * NGRB

        def pump_rc(s_):
            while uidx[0] < len(units) and uidx[0] * pump_span < (s_ + 1) * len(units):
                units[uidx[0]][1]()
                uidx[0] += 1

        for wq_i in range(NQW):
            ubA = (p * NQW + wq_i) * NGR
            ubB = (p * NQW + wq_i) * NGRB
            qq = p * NQW + wq_i
            qoff = wq_i * QW
            aidx = [0]
            avA = [0]
            avB = [0]

            def sc_step(k, g):
                uB = ubB + k
                cbaseB = sum(GRPB[:k])
                w = []
                if uB > 0:
                    w.append(("sEB", uB))
                if g is not None:
                    uA = ubA + g
                    glenA = GRP[g]
                    cbaseA = sum(GRP[:g])
                    if uA > 0:
                        w.append(("sEA", uA))
                if k == 0 and wq_i == 0:
                    w.append(("sRC", p + 1))

                def mk_a(j, qoff=qoff, par=par):
                    c = cbaseA + j
                    return lambda e, j=j, c=c, qoff=qoff, par=par: e.matmul(
                        PX[:, j * 512:(j + 1) * 512],
                        KT[0:64, par * S + c * 128: par * S + (c + 1) * 128],
                        QT[0:64, par * SL + qoff: par * SL + qoff + QW],
                        start=True, stop=True)

                def mk_b(j, qoff=qoff, par=par):
                    c = cbaseB + j
                    return lambda e, j=j, c=c, qoff=qoff, par=par: e.matmul(
                        PY[:, j * 512:(j + 1) * 512],
                        KT[64:128, par * S + c * 128: par * S + (c + 1) * 128],
                        QT[64:128, par * SL + qoff: par * SL + qoff + QW],
                        start=True, stop=True, tile_position=(64, 0))
                seq = []
                if g is None:
                    seq = [(mk_b(0), "B"), (mk_b(1), "B")]
                else:
                    a_mms = [(mk_a(j), "A") for j in range(glenA)]
                    b_mms = [(mk_b(0), "B"), (mk_b(1), "B")]
                    seq = []
                    for i in range(max(len(a_mms), len(b_mms))):
                        if i < len(a_mms):
                            seq.append(a_mms[i])
                        if i < len(b_mms):
                            seq.append(b_mms[i])
                lastA = max((i for i, s_ in enumerate(seq) if s_[1] == "A"), default=-1)
                lastB = max(i for i, s_ in enumerate(seq) if s_[1] == "B")

                def bundle(e, seq=seq, lastA=lastA, lastB=lastB):
                    last = None
                    for i, (fn, _) in enumerate(seq):
                        last = fn(e)
                        if i == lastA:
                            last.then_inc(sem["sSA"], 1)
                        if i == lastB:
                            last.then_inc(sem["sSB"], 1)
                    return last
                P.op("pe", bundle, waits=w, inc=None,
                     sim_incs=([("sSA", 1)] if g is not None else []) + [("sSB", 1)])
                wB = [("sSB", uB + 1)]
                if uB >= 4:
                    wB.append(("sAVB", uB - 3))
                P.op("act", lambda e, buf=uB % 4: e.activation(
                    PTB[:, buf * 1024:buf * 1024 + 1024], PY[:, 0:1024], AF.Exp,
                    bias=CB[:, C_SHIFT:C_SHIFT + 1], scale=CB[:, C_SCL:C_SCL + 1]),
                    waits=wB, inc=("sEB", 1))
                if g is not None:
                    fd = glenA * 512
                    wA = [("sSA", uA + 1)]
                    if uA >= 4:
                        wA.append(("sAVA", uA - 3))
                    P.op("act", lambda e, fd=fd, buf=uA % 4: e.activation(
                        PTA[:, buf * 1536:buf * 1536 + fd], PX[:, 0:fd], AF.Exp,
                        bias=CB[:, C_SHIFT:C_SHIFT + 1], scale=CB[:, C_SCL:C_SCL + 1]),
                        waits=wA, inc=("sEA", 1))

            for k in range(NGRB):
                g = None
                if aidx[0] * NGRB < (k + 1) * NGR:
                    g = aidx[0]
                    aidx[0] += 1
                sc_step(k, g)
                while aidx[0] * NGRB < (k + 1) * NGR:
                    sc_step_extra = aidx[0]
                    aidx[0] += 1
                    raise AssertionError("unexpected extra A group")
                if k == 3:
                    for f in pending_epi:
                        f()
                    pending_epi.clear()
                if k in (8, 14, 20, 26) and pending_a1:
                    pending_a1.pop(0)()
                if k >= 3:
                    while avB[0] <= k - 3:
                        _avu(p, qq, GRPB, avB[0], ubB + avB[0], PTB, 1024, POB, 64, "sEB", "sAVB")
                        avB[0] += 1
                    while (avA[0] + 3) * NGRB < (k + 1) * NGR:
                        _avu(p, qq, GRP, avA[0], ubA + avA[0], PTA, 1536, POA, 0, "sEA", "sAVA")
                        avA[0] += 1
                pw_i = wq_i - 1 if p == 0 else wq_i
                if pw_i >= 0:
                    pump_rc(pw_i * NGRB + k)
            while avA[0] < NGR:
                _avu(p, qq, GRP, avA[0], ubA + avA[0], PTA, 1536, POA, 0, "sEA", "sAVA")
                avA[0] += 1
            while avB[0] < NGRB:
                _avu(p, qq, GRPB, avB[0], ubB + avB[0], PTB, 1024, POB, 64, "sEB", "sAVB")
                avB[0] += 1

            def epi(qq=qq, qoff=qoff, p=p, wq_i=wq_i):
                P.op("dve", lambda e: e.tensor_copy(OEA[0:65, :], POA[0:65, :]),
                     waits=[("sAVA", (qq + 1) * NGR)], inc=("sOEA", 1), flush=True)
                P.op("dve", lambda e: e.tensor_copy(OEB[0:65, :], POB[0:65, :]),
                     waits=[("sAVB", (qq + 1) * NGRB)], inc=("sOEB", 1), flush=True)

                def tra(e):
                    last = None
                    for j in range(4):
                        last = e.transpose(POA[:, j * 128:(j + 1) * 128], OEA[:, j * 128:(j + 1) * 128], EYEF[:])
                    return last
                P.op("pe", tra, waits=[("sOEA", qq + 1)], inc=("sTRA", 1))

                def trb_(e):
                    last = None
                    for j in range(4):
                        last = e.transpose(POB[:, j * 128:(j + 1) * 128], OEB[:, j * 128:(j + 1) * 128], EYEF[:])
                    return last
                P.op("pe", trb_, waits=[("sOEB", qq + 1)], inc=("sTRB", 1))

                w = [("sTRA", qq + 1), ("sTRB", qq + 1)]
                if qq > 0:
                    w.append(("sOSW", 16 * 8 * qq))

                def dv(e):
                    for j in range(4):
                        e.reciprocal(RD[:, j:j + 1], POA[:, j * 128 + 64:j * 128 + 65])
                    for j in range(4):
                        e.reciprocal(RD[:, 4 + j:5 + j], POB[:, j * 128:j * 128 + 1])
                    e.drain()
                    last = None
                    for j in range(4):
                        e.tensor_scalar(OTD[:, j * 64:(j + 1) * 64], POA[:, j * 128:j * 128 + 64],
                                        RD[:, j:j + 1], None, ALU.mult)
                        last = e.tensor_scalar(OTD2[:, j * 64:(j + 1) * 64], POB[:, j * 128 + 1:j * 128 + 65],
                                               RD[:, 4 + j:5 + j], None, ALU.mult)
                    return last
                P.op("dve", dv, waits=w, inc=("sDVA", 1), flush=True)
                for j in range(4):
                    dst = d_osc[qoff + j * 128: qoff + (j + 1) * 128, (2 * p) * 64:(2 * p + 1) * 64]
                    P.op("pool", lambda e, d=dst, j=j: e.dma_start(d, OTD[:, j * 64:(j + 1) * 64]),
                         waits=[("sDVA", qq + 1)], inc=("sOSW", 16))
                for j in range(4):
                    dst = d_osc[qoff + j * 128: qoff + (j + 1) * 128, (2 * p + 1) * 64:(2 * p + 2) * 64]
                    P.op("pool", lambda e, d=dst, j=j: e.dma_start(d, OTD2[:, j * 64:(j + 1) * 64]),
                         waits=[("sDVA", qq + 1)], inc=("sOSW", 16))
                if p == NP - 1:
                    for ti in range(wq_i * 4, wq_i * 4 + 4):
                        pending_a1.append(lambda ti=ti: emit_a1(ti))
            pending_epi.append(epi)
        pump_rc(10 ** 9)

    if PHASES == "all":
        ph1(rc_units(0))
        for p in range(NP):
            emit_pair(p)
        for f in pending_epi:
            f()
        pending_epi.clear()
        for f in pending_a1:
            f()
        pending_a1.clear()

    def emit_a2(i):
        buf = i % 2
        w = [("s3X1", 16 * 16)]
        if i >= 2:
            w.append(("s3N", i - 1))
        P.op("sp", lambda e, i=i, b=buf: e.dma_start(X1[:, b * DM:(b + 1) * DM],
                                                     d_x1sc[i * 128:(i + 1) * 128, :]),
             waits=w, inc=("sOSR", 16))
        wap = [("sOSR", 16 * (17 + i))]
        if i >= 2:
            wap.append(("s3TD", 16 * (i - 1)))
        P.op("dve", lambda e, i=i, b=buf: e.tensor_scalar(X1N[:, b * DM:(b + 1) * DM],
                                                          X1[:, b * DM:(b + 1) * DM],
                                                          MVS3[:, i:i + 1], RSS3[:, i:i + 1],
                                                          ALU.subtract, ALU.mult),
             waits=wap, inc=None, flush=False)
        P.op("dve", lambda e: e.drain(), inc=("s3N", 1))
        wtd = [("s3N", i + 1)]
        if i >= 4:
            wtd.append(("s3S", i - 3))

        def t3d(e, b2=buf, b4=i % 4):
            dst = X1NT[:, b4 * DM:(b4 + 1) * DM].rearrange("p (j t) -> p j t", j=8)
            return e.dma_start_transpose(dst, X1N[:, b2 * DM:(b2 + 1) * DM])
        P.op("sp", t3d, waits=wtd, inc=("s3TD", 16))

        def smm(e, b4=i % 4, sl=i % 2):
            ps = SPARE[0:64, sl * 128:(sl + 1) * 128]
            for dc in range(8):
                e.matmul(ps, FFNW[:, dc * 64:(dc + 1) * 64],
                         X1NT[:, b4 * DM + dc * 128:b4 * DM + (dc + 1) * 128],
                         start=(dc == 0), stop=False)
            return e.matmul(ps, BIN[:], ONESR[:, 0:128], start=False, stop=True)
        wsm = [("s3TD", 16 * (i + 1))]
        if i >= 2:
            wsm.append(("s3S2", i - 1))
        P.op("pe", smm, waits=wsm, inc=("s3S", 1))
        wcp = [("s3S", i + 1)]
        if i >= 8:
            wcp.append(("s3M", 1))
        P.op("dve", lambda e, i=i, sl=i % 2: e.tensor_copy(SALL[:, (i % 8) * 128:(i % 8 + 1) * 128],
                                                           SPARE[0:64, sl * 128:(sl + 1) * 128]),
             waits=wcp, inc=("s3S2", 1), flush=True)

    def emit_mid(half):
        def mid(e):
            for h in range(16):
                e.tensor_scalar(TM3[:], SALL[:, 0:1024], FFS[0:64, h:h + 1], FFS[0:64, 16 + h:17 + h],
                                ALU.mult, ALU.add)
                e.tensor_scalar(TM3[:], TM3[:], CB[0:64, C_ZERO:C_ZERO + 1], None, ALU.max)
                if h == 0:
                    e.tensor_scalar(G3[:], TM3[:], FFS[0:64, 32:33], FFS[0:64, 48:49],
                                    ALU.mult, ALU.add)
                else:
                    e.scalar_tensor_tensor(G3[:], TM3[:], FFS[0:64, 32 + h:33 + h], G3[:],
                                           ALU.mult, ALU.add)
                e.drain()
            e.memset(GB3[64:65, :], 1.0)
            return e.tensor_copy(GB3[0:64, :], G3[:])
        wmid = [("s3S2", half * 8 + 8)]
        if half == 0:
            wmid += [("sEA", NP * NQW * NGR), ("sEB", NP * NQW * NGRB)]
        else:
            wmid.append(("s3O", 8))
        P.op("dve", mid, waits=wmid, inc=("s3M", 1), flush=True)

    def emit_c(i):
        buf = i % 2
        w = [("s3M", i // 8 + 1)]
        if i >= 2:
            w.append(("s3F", i - 1))
        P.op("sp", lambda e, i=i, b=buf: e.dma_start(XQT[:, b * DM:(b + 1) * DM],
                                                     d_x1sc[i * 128:(i + 1) * 128, :]),
             waits=w, inc=("sXQL", 16))
        PP = PX[:, 0:1024] if i % 2 == 0 else PY[:, 0:1024]

        def omm2(e, i=i, PP=PP):
            last = None
            for h2 in range(2):
                last = e.matmul(PP[:, h2 * 512:(h2 + 1) * 512],
                                GB3[:, (i % 8) * 128:(i % 8 + 1) * 128],
                                WOUT[:, h2 * 512:(h2 + 1) * 512],
                                start=True, stop=True)
            return last
        wo = [("s3M", i // 8 + 1)]
        if i >= 2:
            wo.append(("s3F", i - 1))
        P.op("pe", omm2, waits=wo, inc=("s3O", 1))
        P.op("dve", lambda e, PP=PP: e.bn_stats(ST3[:, 0:6], PP[:, 0:512]), waits=[("s3O", i + 1)])
        P.op("dve", lambda e, PP=PP: e.bn_stats(ST3[:, 6:12], PP[:, 512:1024]))
        P.op("dve", lambda e: (e.drain(), e.bn_aggr(MV3[:], ST3[:]))[1], inc=("s3ST", 1), flush=True)
        P.op("act", lambda e: e.activation(T1[:], MV3[:, 1:2], AF.Ln,
                                           bias=CB[:, C_EPS:C_EPS + 1], scale=CB[:, C_ONE:C_ONE + 1]),
             waits=[("s3ST", 32 + i + 1)])
        P.op("act", lambda e: e.activation(RS[:, 0:1], T1[:], AF.Exp,
                                           bias=CB[:, C_ZERO:C_ZERO + 1], scale=CB[:, C_NHALF:C_NHALF + 1]),
             inc=("s3RS", 1), flush=True)
        P.op("dve", lambda e, PP=PP: e.tensor_scalar(TMP3[:], PP[:, 0:1024], MV3[:, 0:1], RS[:, 0:1],
                                                     ALU.subtract, ALU.mult),
             waits=[("s3RS", 32 + i + 1)])
        wyt = [("sYW", 16 * i)] if i >= 1 else []
        P.op("dve", lambda e, b=buf: e.tensor_add(YT[:], TMP3[:], XQT[:, b * DM:(b + 1) * DM]),
             waits=wyt + [("sXQL", 16 * (17 + i))], inc=("s3F", 1), flush=True)
        P.op("pool", lambda e, i=i: e.dma_start(d_y[i * 128:(i + 1) * 128, :], YT[:]),
             waits=[("s3F", i + 1)], inc=("sYW", 16))

    if PHASES == "all":
        for i in range(8):
            emit_a2(i)
        emit_mid(0)
        for j in range(8):
            emit_a2(8 + j)
            emit_c(j)
        emit_mid(1)
        for j in range(8, 16):
            emit_c(j)

    if PHASES != "all":
        ph1()
    FINAL_WAIT = ("sYW", 16 * 16) if PHASES == "all" else ("sXWR", XWR_ALL)
    P.op("sp", lambda e: e.nop(), waits=[FINAL_WAIT])
    if DEBUG_DUMPS:
        d_dosc = nc.dram_tensor("dbg_osc", [SL, DM], F32, kind="ExternalOutput").ap()
        d_dxlt = nc.dram_tensor("dbg_xlt", [8, 128, 1024], BF16, kind="ExternalOutput").ap()
        d_dxlq = nc.dram_tensor("dbg_xltq", [8, 128, 1024], BF16, kind="ExternalOutput").ap()
        d_dkt = nc.dram_tensor("dbg_kt", [128, S], BF16, kind="ExternalOutput").ap()
        d_dqt = nc.dram_tensor("dbg_qt", [128, SL], BF16, kind="ExternalOutput").ap()
        d_dva = nc.dram_tensor("dbg_vau", [128, 10 * 129], BF16, kind="ExternalOutput").ap()
        P.op("sp", lambda e: e.dma_start(d_dosc, d_osc[:]), waits=[FINAL_WAIT], inc=("sDBG", 16))
        P.op("sp", lambda e: e.dma_start(d_dxlt, d_xlt[:, :, 0:1024]), inc=("sDBG", 16))
        P.op("sp", lambda e: e.dma_start(d_dxlq, d_xlt[:, :, 0:1024]), inc=("sDBG", 16))
        P.op("sp", lambda e: e.dma_start(d_dkt, KT[:, S:2 * S]), inc=("sDBG", 16))
        P.op("sp", lambda e: e.dma_start(d_dqt, QT[:, SL:2 * SL]), inc=("sDBG", 16))
        P.op("sp", lambda e: e.dma_start(d_dva, VAU[:, NCH * 129:NCH * 129 + 10 * 129]), inc=("sDBG", 16))
        P.op("sp", lambda e: e.nop(), waits=[("sDBG", 96)])

    build_nc.last_prog = P
    with nc.Block() as block:
        def body_for(name):
            def body(eng):
                for waits, fn, inc, flush in P.items[name]:
                    for (s, v) in waits:
                        if v > 0:
                            eng.wait_ge(sem[s], v)
                    instr = fn(eng)
                    if inc is not None:
                        if flush:
                            eng.drain().then_inc(sem[inc[0]], inc[1])
                        else:
                            instr.then_inc(sem[inc[0]], inc[1])
            return body
        block.sync(body_for("sp"))
        block.tensor(body_for("pe"))
        block.scalar(body_for("act"))
        block.vector(body_for("dve"))
        block.gpsimd(body_for("pool"))

    return nc


def _emit_av(P, p, wq_i, g, u, PT, VAU, PO, voff, sE, sAV, sDV):
    glen = GRP[g]
    cbase = sum(GRP[:g])
    buf = u % 2
    qq = p * NQW + wq_i
    waits = [(sE, u + 1)]
    if g == 0 and qq > 0:
        waits.append((sDV, qq))
    def av(e, glen=glen, cbase=cbase, buf=buf, g=g, voff=voff):
        last = None
        for j in range(glen):
            c = cbase + j
            last = e.matmul(PO[0:65, 0:QW],
                            VAU[:, c * 129 + voff: c * 129 + voff + 65],
                            PT[:, buf * 1536 + j * 512: buf * 1536 + (j + 1) * 512],
                            start=(g == 0 and j == 0), stop=(g == NGR - 1 and j == glen - 1))
        return last
    P.op("pe", av, waits=waits, inc=(sAV, 1))


_NC_CACHE = {}


def _get_nc():
    if "nc" not in _NC_CACHE:
        _NC_CACHE["nc"] = build_nc()
    return _NC_CACHE["nc"]


def kernel(L0_attn_x, L0_attn_wq_w, L0_attn_wq_b, L0_attn_wk_w, L0_attn_wk_b,
           L0_attn_wv_w, L0_attn_wv_b, L0_ffn_Win, L0_ffn_bin_, L0_ffn_W1,
           L0_ffn_b1, L0_ffn_W2, L0_ffn_b2, L0_ffn_Wout, L0_ffn_bout):
    x = np.asarray(L0_attn_x, dtype=np.float32)
    wq = np.asarray(L0_attn_wq_w, np.float32)
    wk = np.asarray(L0_attn_wk_w, np.float32)
    wv = np.asarray(L0_attn_wv_w, np.float32)
    bq = np.asarray(L0_attn_wq_b, np.float32)
    bk = np.asarray(L0_attn_wk_b, np.float32)
    bv = np.asarray(L0_attn_wv_b, np.float32)

    wqT = np.ascontiguousarray(wq.T)
    wkT = np.ascontiguousarray(wk.T)
    wvT = np.ascontiguousarray(wv.T)

    wkv_h = np.zeros((8, 128, NP * PW), np.float32)
    bkv_h = np.zeros((1, NP * PW), np.float32)
    for p in range(NP):
        hA, hB = 2 * p, 2 * p + 1
        base = p * PW
        colsK = slice(base, base + 128)
        colsVA = slice(base + 128, base + 192)
        colsVB = slice(base + 193, base + 257)
        colsQ = slice(base + 257, base + 385)
        for dc in range(8):
            din = slice(dc * 128, (dc + 1) * 128)
            wkv_h[dc, :, colsK] = wkT[din, hA * 64:hB * 64 + 64]
            wkv_h[dc, :, colsVA] = wvT[din, hA * 64:(hA + 1) * 64]
            wkv_h[dc, :, colsVB] = wvT[din, hB * 64:(hB + 1) * 64]
            wkv_h[dc, :, colsQ] = wqT[din, hA * 64:hB * 64 + 64]
        bkv_h[0, colsK] = bk[hA * 64:hB * 64 + 64]
        bkv_h[0, colsVA] = bv[hA * 64:(hA + 1) * 64]
        bkv_h[0, base + 192] = 1.0
        bkv_h[0, colsVB] = bv[hB * 64:(hB + 1) * 64]
        bkv_h[0, colsQ] = bq[hA * 64:hB * 64 + 64]

    theta = (10000.0 ** (-(np.arange(32, dtype=np.float64) / 32.0))).astype(np.float64)
    pos = np.arange(S, dtype=np.float64)
    ang = np.outer(pos, theta)
    cosf = np.cos(ang).astype(np.float32)
    sinf = np.sin(ang).astype(np.float32)

    def mk_cs(tokens):
        nb = len(tokens) // 128
        c = np.zeros((128, nb * 64), np.float32)
        s = np.zeros((128, nb * 64), np.float32)
        for b in range(nb):
            tt = tokens[b * 128:(b + 1) * 128]
            cc = cosf[tt]
            ss = sinf[tt]
            c[:, b * 64:b * 64 + 32] = cc
            c[:, b * 64 + 32:b * 64 + 64] = cc
            s[:, b * 64:b * 64 + 32] = ss
            s[:, b * 64 + 32:b * 64 + 64] = ss
        return np.concatenate([c, s], axis=1)

    csk_h = mk_cs(np.arange(S))

    cb_h = np.zeros((128, 8), np.float32)
    cb_h[:, C_SHIFT] = -SHIFT
    cb_h[:, C_SCL] = 1.0 / np.sqrt(HD)
    cb_h[:, C_ONE] = 1.0
    cb_h[:, C_ZERO] = 0.0
    cb_h[:, C_NHALF] = -0.5
    cb_h[:, C_EPS] = EPS

    eyef_h = np.eye(128).astype(np.float32)

    win_h = np.asarray(L0_ffn_Win, np.float32).reshape(8, 128, 64)
    wout_h = np.vstack([np.asarray(L0_ffn_Wout, np.float32),
                        np.asarray(L0_ffn_bout, np.float32)[None, :]])
    ffs_h = np.zeros((64, 50), np.float32)
    ffs_h[:, 0:16] = np.asarray(L0_ffn_W1, np.float32)
    ffs_h[:, 16:32] = np.asarray(L0_ffn_b1, np.float32)
    ffs_h[:, 32:48] = np.asarray(L0_ffn_W2, np.float32)
    ffs_h[:, 48] = np.asarray(L0_ffn_b2, np.float32)
    fbr_h = np.zeros((1, DM + 64), np.float32)
    fbr_h[0, 0:64] = np.asarray(L0_ffn_bin_, np.float32)
    fbr_h[0, 64:] = np.asarray(L0_ffn_bout, np.float32)

    shared = {
        "wkv": wkv_h, "bkv": bkv_h,
        "cbv": cb_h, "eyef": eyef_h,
        "ffw_in": win_h, "ffw_out": wout_h, "ffs": ffs_h, "fbr": fbr_h,
    }

    in_maps = []
    for c in range(NC):
        b = c // 4
        sl = (c % 4) * SL
        rot = np.r_[np.arange(sl, sl + SL), np.arange(0, sl), np.arange(sl + SL, S)]
        m = dict(shared)
        m["x"] = np.ascontiguousarray(x[b][rot])
        m["csk"] = mk_cs(rot)
        in_maps.append(m)

    nc = _get_nc()
    res = run_bass_kernel_spmd(nc, in_maps, core_ids=list(range(NC)))
    global LAST_RES
    LAST_RES = res
    out = np.empty((B, S, DM), np.float32)
    for c in range(NC):
        b = c // 4
        sl = (c % 4) * SL
        out[b, sl:sl + SL] = np.asarray(res.results[c]["y"], np.float32)
    return out


# version 18
# speedup vs baseline: 1.3159x; 1.3159x over previous
import numpy as np
import ml_dtypes
import concourse.bass as bass
import concourse.mybir as mybir
from concourse.bass_utils import run_bass_kernel_spmd

F32 = mybir.dt.float32
BF16 = mybir.dt.bfloat16
AF = mybir.ActivationFunctionType
ALU = mybir.AluOpType
BF = ml_dtypes.bfloat16

B, S, DM, H, HD = 2, 8192, 1024, 16, 64
NC = 8
SL = S // 4
NP = H // 2
NB = S // 128
NQB = SL // 128
NCH = S // 128
QW = 512
NQW = SL // QW
GRP = [3] * 21 + [1]
NGR = len(GRP)
SHIFT = 4.0
EPS = 1e-5
KVC = 257
PW = 385
RBT = NB
NSLAB = RBT // 8
KTD_PP = 10


def cumktd(rp, s):
    if s < 0:
        return rp * KTD_PP
    return rp * KTD_PP + (2 * (s + 1) if s <= 1 else s + 3)

C_SHIFT, C_SCL, C_ONE, C_ZERO, C_NHALF, C_EPS = 0, 1, 2, 3, 4, 5

ENGS = ("sp", "pe", "act", "dve", "pool")
DEBUG_DUMPS = False
DEBUG_LN1_BYPASS = False
PHASES = "all"
LAST_RES = None


class Prog:

    def __init__(self):
        self.items = {e: [] for e in ENGS}
        self.sim_incs = {}

    def op(self, eng, fn, waits=(), inc=None, flush=False, sim_incs=None):
        if sim_incs:
            self.sim_incs[(eng, len(self.items[eng]))] = list(sim_incs)
        self.items[eng].append((tuple(waits), fn, inc, flush))


def build_nc():
    nc = bass.Bass("TRN2", target_bir_lowering=False)

    d_x = nc.dram_tensor("x", [S, DM], F32, kind="ExternalInput").ap()
    d_wkv = nc.dram_tensor("wkv", [8, 128, NP * PW], F32, kind="ExternalInput").ap()
    d_bkv = nc.dram_tensor("bkv", [1, NP * PW], F32, kind="ExternalInput").ap()
    d_csk = nc.dram_tensor("csk", [128, 2 * NB * 64], F32, kind="ExternalInput").ap()
    d_cb = nc.dram_tensor("cbv", [128, 8], F32, kind="ExternalInput").ap()
    d_eyef = nc.dram_tensor("eyef", [128, 128], F32, kind="ExternalInput").ap()
    d_win = nc.dram_tensor("ffw_in", [8, 128, 64], F32, kind="ExternalInput").ap()
    d_wout = nc.dram_tensor("ffw_out", [65, DM], F32, kind="ExternalInput").ap()
    d_ffs = nc.dram_tensor("ffs", [64, 50], F32, kind="ExternalInput").ap()
    d_fbr = nc.dram_tensor("fbr", [1, DM + 64], F32, kind="ExternalInput").ap()

    d_xlt = nc.dram_tensor("xlt", [8, 128, S], BF16, kind="Internal").ap()
    d_osc = nc.dram_tensor("osc", [SL, DM], F32, kind="Internal").ap()
    d_x1sc = nc.dram_tensor("x1sc", [SL, DM], F32, kind="Internal").ap()
    d_y = nc.dram_tensor("y", [SL, DM], F32, kind="ExternalOutput").ap()

    A = nc.alloc_sbuf_tensor
    ST = A("ST", [128, 48], F32)
    MV = A("MV", [128, 8], F32)
    T1 = A("T1", [128, 1], F32)
    RS = A("RS", [128, 4], F32)
    BKV = A("BKV", [1, NP * PW], BF16)
    ONESR = A("ONESR", [1, 128], BF16)
    FFNW = A("FFNW", [128, 8 * 64], BF16)
    WOUT = A("WOUT", [65, DM], BF16)
    FFS = A("FFS", [64, 50], F32)
    BIN = A("BIN", [1, 64], BF16)
    CB = A("CB", [128, 8], F32)
    EYEF = A("EYEF", [128, 128], F32)
    KT = A("KT", [128, 2 * S], BF16)
    QT = A("QT", [128, 2 * SL], BF16)
    VAU = A("VAU", [128, 2 * NCH * 129], BF16)
    TRS = A("TRS", [128, 256], F32)
    KTOK = A("KTOK", [128, 8 * 128], BF16)
    QTOK = A("QTOK", [128, 8 * 128], BF16)
    PTA = A("PTA", [128, 4 * 1536], BF16)
    PTB = A("PTB", [128, 4 * 1024], BF16)
    OEA = A("OEA", [128, 512], F32)
    OEB = A("OEB", [128, 512], F32)
    RD = A("RD", [128, 8], F32)
    OTD = A("OTD", [128, 256], F32)
    OTD2 = A("OTD2", [128, 256], F32)
    XB = A("XB", [128, 2 * 8192], BF16)
    XW = A("XW", [128, 4 * DM], F32)
    XLTS = A("XLTS", [128, 2 * 4 * 1024], BF16)
    WST = A("WST", [128, 8 * PW], F32)
    CSB = A("CSB", [128, 2 * 1024], F32)
    XLN = A("XLN", [128, 2 * DM], BF16)
    WKVP = A("WKVP", [128, 8 * PW], BF16)
    r1base = nc.lookup_mloc(XB).addr
    _off = [r1base]

    def AO(name, shape, dtype):
        nbytes = shape[-1] * (4 if dtype == F32 else 2)
        nbytes = (nbytes + 31) // 32 * 32
        t = nc.alloc_sbuf_tensor_at(name, shape, dtype, offset=_off[0])
        _off[0] += nbytes
        return t

    OSCT = AO("OSCT", [128, 2 * DM], F32)
    XQT = AO("XQT", [128, 2 * DM], F32)
    X1 = AO("X1", [128, 4 * DM], F32)
    TMP3 = AO("TMP3", [128, DM], F32)
    X1N = AO("X1N", [128, 2 * DM], BF16)
    X1NT = AO("X1NT", [128, 4 * DM], BF16)
    TM3 = AO("TM3", [64, 1024], F32)
    G3 = AO("G3", [64, 1024], F32)
    GB3 = AO("GB3", [65, 1024], BF16)
    YT = AO("YT", [128, DM], F32)
    MVS3 = AO("MVS3", [128, 16], F32)
    ST3 = AO("ST3", [128, 12], F32)
    MV3 = AO("MV3", [128, 2], F32)
    RSS3 = AO("RSS3", [128, 16], F32)
    SALL = AO("SALL", [64, 2048], F32)
    assert _off[0] <= r1base + 78 * 1024, "phase-3 overlay exceeds R1 region"

    PX = nc.alloc_psum_tensor("PX", [128, 1536], F32)
    PY = nc.alloc_psum_tensor("PY", [128, 1536], F32)
    POA = nc.alloc_psum_tensor("POA", [128, 512], F32)
    POB = nc.alloc_psum_tensor("POB", [128, 512], F32)

    def pslot(s):
        if s < 3:
            return PX[:, s * 512:(s + 1) * 512]
        return PY[:, (s - 3) * 512:(s - 2) * 512]

    sem = {}
    for name in ("sXL sST sSTA sRS sXA sXT sXE sXWR sWL sWC "
                 "sXBL sCSL sPJ sRP sKTB sKTE sKTD sRC "
                 "sSA sSB sEA sEB sAVA sAVB sOEA sOEB sTRA sTRB sDVA sDVB sOSW "
                 "sOSR sXQL s3ST s3RS s3X1 s3A1 s3S2 s3N s3T s3TD sX1E s3S s3M s3O s3OE s3TT s3F sYW sDBG").split():
        sem[name] = nc.alloc_semaphore(name)

    P = Prog()

    wl = [0]
    wc = [0]

    def load(dst, src, eng="sp"):
        P.op(eng, lambda e, d=dst, s=src: e.dma_start(d, s), inc=("sWL", 16))
        wl[0] += 16

    load(CB[:], d_cb)
    load(EYEF[:], d_eyef)
    load(FFS[:], d_ffs)

    P.op("dve", lambda e: e.memset(ONESR[:], 1.0), inc=("sWC", 1)); wc[0] += 1

    P.op("sp", lambda e: e.dma_start(WST[0:1, 0:NP * PW], d_bkv),
         waits=[("sWC", wc[0])], inc=("sWL", 16)); wl[0] += 16
    P.op("dve", lambda e: e.tensor_copy(BKV[:], WST[0:1, 0:NP * PW]),
         waits=[("sWL", wl[0])], inc=("sWC", 1)); wc[0] += 1
    for dc in range(8):
        P.op("sp", lambda e, dc=dc: e.dma_start(WST[:, 0:64], d_win[dc]),
             waits=[("sWC", wc[0])], inc=("sWL", 16)); wl[0] += 16
        P.op("dve", lambda e, dc=dc: e.tensor_copy(FFNW[:, dc * 64:(dc + 1) * 64], WST[:, 0:64]),
             waits=[("sWL", wl[0])], inc=("sWC", 1)); wc[0] += 1
    P.op("sp", lambda e: e.dma_start(WST[0:65, 0:DM], d_wout),
         waits=[("sWC", wc[0])], inc=("sWL", 16)); wl[0] += 16
    P.op("dve", lambda e: e.tensor_copy(WOUT[:], WST[0:65, 0:DM]),
         waits=[("sWL", wl[0])], inc=("sWC", 1)); wc[0] += 1
    P.op("sp", lambda e: e.dma_start(WST[0:1, 0:DM + 64], d_fbr),
         waits=[("sWC", wc[0])], inc=("sWL", 16)); wl[0] += 16
    P.op("dve", lambda e: e.tensor_copy(BIN[:], WST[0:1, 0:64]),
         waits=[("sWL", wl[0])], inc=("sWC", 1)); wc[0] += 1
    W_ALL = wl[0]
    WC_ALL = wc[0]

    def ph1():
        NT = NB

        def emit_load(t):
            b3 = t % 4
            w = [("sSTA", t - 3)] if t >= 4 else []
            if t == 0:
                w.append(("sWL", W_ALL))
            P.op("sp", lambda e, t=t, b=b3: e.dma_start(XW[:, b * DM:(b + 1) * DM],
                                                        d_x[t * 128:(t + 1) * 128, :]),
                 waits=w, inc=("sXL", 16))

        def emit_stats(t):
            buf = t % 4
            w2 = [("sXL", 16 * (t + 1))]
            if t >= 4:
                w2.append(("sRS", t - 3))
            P.op("dve", lambda e, b=buf: e.bn_stats(ST[:, b * 6:(b + 1) * 6], XW[:, b * DM:b * DM + 512]),
                 waits=w2)
            P.op("dve", lambda e, b=buf: e.bn_stats(ST[:, 24 + b * 6:24 + (b + 1) * 6],
                                                    XW[:, b * DM + 512:b * DM + 1024]))

        def emit_aggr(t):
            buf = t % 4
            ap_in = ST[:].rearrange("p (h x) -> p h x", h=2)[:, :, 6 * buf:6 * buf + 6]
            P.op("dve", lambda e, b=buf, a=ap_in: e.bn_aggr(MV[:, 2 * b:2 * b + 2], a),
                 inc=("sST", 1), flush=True)

        def emit_rstd(t):
            buf = t % 4
            P.op("act", lambda e, b=buf: e.activation(T1[:], MV[:, 2 * b + 1:2 * b + 2], AF.Ln,
                                                      bias=CB[:, C_EPS:C_EPS + 1], scale=CB[:, C_ONE:C_ONE + 1]),
                 waits=[("sST", t + 1)])
            P.op("act", lambda e, b=buf: e.activation(RS[:, b:b + 1], T1[:], AF.Exp,
                                                      bias=CB[:, C_ZERO:C_ZERO + 1],
                                                      scale=CB[:, C_NHALF:C_NHALF + 1]),
                 inc=("sRS", 1))

        def emit_apply(t):
            buf = t % 2
            bs = t % 4
            w = [("sRS", t + 1)]
            if t >= 2:
                w.append(("sXT", 16 * (t - 1)))
            P.op("dve", lambda e, b=buf, bs=bs: e.tensor_scalar(
                XLN[:, b * DM:(b + 1) * DM], XW[:, bs * DM:(bs + 1) * DM],
                MV[:, 2 * bs:2 * bs + 1], RS[:, bs:bs + 1], ALU.subtract, ALU.mult),
                waits=w, inc=("sSTA", 1))

        def emit_trw(t):
            buf = t % 2
            g2 = (t // 4) % 2
            w = [("sSTA", t + 1)]
            if t >= 8:
                w.append(("sXWR", 16 * (t // 4 - 1)))

            def trd(e, b=buf, tt=t, g2=g2):
                dst = XLTS[:, g2 * 4096:(g2 + 1) * 4096].rearrange(
                    "p (j u) -> p j u", j=8)[:, :, (tt % 4) * 128:(tt % 4 + 1) * 128]
                return e.dma_start_transpose(dst, XLN[:, b * DM:(b + 1) * DM])
            P.op("sp", trd, waits=w, inc=("sXT", 16))
            if t % 4 == 3:
                sl4 = t // 4
                def wr(e, sl=sl4, g2=g2):
                    dst = d_xlt[:, :, sl * 512:(sl + 1) * 512].rearrange("j p t -> p j t")
                    src = XLTS[:, g2 * 4096:(g2 + 1) * 4096].rearrange("p (j u) -> p j u", j=8)
                    return e.dma_start(dst, src)
                P.op("pool", wr, waits=[("sXT", 16 * (t + 1))], inc=("sXWR", 16))

        for lt in range(3):
            emit_load(lt)
        for t in range(NT + 2):
            if t < NT:
                emit_stats(t)
            if 1 <= t <= NT:
                emit_aggr(t - 1)
                emit_rstd(t - 1)
            if 2 <= t <= NT + 1:
                emit_apply(t - 2)
                emit_trw(t - 2)
            if t + 3 < NT:
                emit_load(t + 3)
    ph1()
    XWR_ALL = 16 * (NB // 4)

    GRPB = [2] * 32
    NGRB = len(GRPB)
    slab_hist = {0: [], 1: []}
    SPARE = PY[:, 1024:1536]

    def emit_a1(i):
        buf = i % 2
        qwi = i // 4
        qq7 = NP * NQW - NQW + qwi
        w = [("sOSW", 16 * 8 * (qq7 + 1))]
        if i >= 2:
            w.append(("s3X1", 16 * (i - 1)))
        P.op("sp", lambda e, i=i, b=buf: e.dma_start(OSCT[:, b * DM:(b + 1) * DM],
                                                     d_osc[i * 128:(i + 1) * 128, :]),
             waits=w, inc=("sOSR", 16))
        wx = [("s3X1", 16 * (i - 1))] if i >= 2 else []
        P.op("sp", lambda e, i=i, b=buf: e.dma_start(XQT[:, b * DM:(b + 1) * DM],
                                                     d_x[i * 128:(i + 1) * 128, :]),
             waits=wx, inc=("sXQL", 16))
        P.op("dve", lambda e, b=buf: e.bn_stats(ST3[:, 0:6], OSCT[:, b * DM:b * DM + 512]),
             waits=[("sOSR", 16 * (2 * i + 1)), ("s3RS", 2 * i)])
        P.op("dve", lambda e, b=buf: e.bn_stats(ST3[:, 6:12], OSCT[:, b * DM + 512:b * DM + 1024]))
        P.op("dve", lambda e: (e.drain(), e.bn_aggr(MV3[:], ST3[:]))[1], inc=("s3ST", 1), flush=True)
        P.op("act", lambda e: e.activation(T1[:], MV3[:, 1:2], AF.Ln,
                                           bias=CB[:, C_EPS:C_EPS + 1], scale=CB[:, C_ONE:C_ONE + 1]),
             waits=[("s3ST", 2 * i + 1)])
        P.op("act", lambda e: e.activation(RS[:, 0:1], T1[:], AF.Exp,
                                           bias=CB[:, C_ZERO:C_ZERO + 1], scale=CB[:, C_NHALF:C_NHALF + 1]),
             inc=("s3RS", 1), flush=True)
        P.op("dve", lambda e, b=buf: e.tensor_scalar(TMP3[:], OSCT[:, b * DM:(b + 1) * DM],
                                                     MV3[:, 0:1], RS[:, 0:1], ALU.subtract, ALU.mult),
             waits=[("s3RS", 2 * i + 1), ("sXQL", 16 * (i + 1))])
        P.op("dve", lambda e, b=buf: e.tensor_add(X1[:, b * DM:(b + 1) * DM], TMP3[:],
                                                  XQT[:, b * DM:(b + 1) * DM]),
             inc=("s3A1", 1), flush=True)
        P.op("dve", lambda e, b=buf: e.bn_stats(ST3[:, 0:6], X1[:, b * DM:b * DM + 512]))
        P.op("dve", lambda e, b=buf: e.bn_stats(ST3[:, 6:12], X1[:, b * DM + 512:b * DM + 1024]))
        P.op("dve", lambda e: (e.drain(), e.bn_aggr(MV3[:], ST3[:]))[1], inc=("s3ST", 1), flush=True)
        P.op("act", lambda e: e.activation(T1[:], MV3[:, 1:2], AF.Ln,
                                           bias=CB[:, C_EPS:C_EPS + 1], scale=CB[:, C_ONE:C_ONE + 1]),
             waits=[("s3ST", 2 * i + 2)])
        P.op("act", lambda e, i=i: e.activation(RSS3[:, i:i + 1], T1[:], AF.Exp,
                                                bias=CB[:, C_ZERO:C_ZERO + 1], scale=CB[:, C_NHALF:C_NHALF + 1]),
             inc=("s3RS", 1), flush=True)
        P.op("dve", lambda e, i=i: e.tensor_copy(MVS3[:, i:i + 1], MV3[:, 0:1]),
             waits=[("s3RS", 2 * i + 2)])
        P.op("pool", lambda e, i=i, b=buf: e.dma_start(d_x1sc[i * 128:(i + 1) * 128, :],
                                                       X1[:, b * DM:(b + 1) * DM]),
             waits=[("s3A1", i + 1)], inc=("s3X1", 16))



    def rc_units(rp):
        ring = (rp == 0)
        rb0 = rp * RBT
        units = []

        def w_unit():
            src_kv = d_wkv[:, :, rp * PW:(rp + 1) * PW].rearrange("j p c -> p j c")
            P.op("sp", lambda e, s_=src_kv: e.dma_start(
                WST[:].rearrange("p (j c) -> p j c", j=8), s_),
                 waits=[("sWC", wc[0])], inc=("sWL", 16)); wl[0] += 16
            P.op("dve", lambda e: e.tensor_copy(WKVP[:], WST[:]),
                 waits=[("sWL", wl[0]), ("sPJ", rp * RBT)], inc=("sWC", 1)); wc[0] += 1
        units.append(w_unit)
        wc_pair = wc[0] + 1

        def slab_unit(si):
            def f(si=si):
                buf = (rp * NSLAB + si) % 2
                w = [("sXWR", 16 * min(2 * (si + 1), XWR_ALL // 16))]
                if slab_hist[buf]:
                    w.append(("sPJ", slab_hist[buf][-1]))
                src = d_xlt[:, :, si * 1024:(si + 1) * 1024].rearrange("j p t -> p j t")
                dst = XB[:, buf * 8192:(buf + 1) * 8192].rearrange("p (j t) -> p j t", j=8)
                P.op("sp", lambda e, d=dst, s=src: e.dma_start(d, s), waits=w, inc=("sXBL", 16))
                slab_hist[buf].append(rb0 + min((si + 1) * 8, RBT))
                c_src = d_csk[:, si * 512:(si + 1) * 512]
                s_src = d_csk[:, NB * 64 + si * 512: NB * 64 + (si + 1) * 512]
                wcs = [("sRP", slab_hist[buf][-2])] if len(slab_hist[buf]) >= 2 else []
                P.op("sp", lambda e, s_=c_src, b=buf: e.dma_start(CSB[:, b * 1024:b * 1024 + 512], s_),
                     waits=wcs, inc=("sCSL", 16))
                P.op("sp", lambda e, s_=s_src, b=buf: e.dma_start(CSB[:, b * 1024 + 512:b * 1024 + 1024], s_),
                     waits=[], inc=("sCSL", 16))
            return f

        def block_unit(ib):
            def f(ib=ib):
                rb = rb0 + ib
                hasq = ib < 16
                si = ib // 8
                buf = (rp * NSLAB + si) % 2
                bs = ib % 8
                w = [("sXBL", 16 * (rp * NSLAB + si + 1))]
                if ring:
                    if rb >= 5:
                        w.append(("sRP", rb - 4))
                else:
                    w.append(("sRP", rb))
                if ib == 0:
                    w.append(("sWC", wc_pair))
                slot = pslot(1 + rb % 5) if ring else SPARE
                ncols = PW if hasq else KVC

                def proj(e, rp=rp, buf=buf, bs=bs, slot=slot, ncols=ncols):
                    for dc in range(8):
                        lhsT = XB[:, buf * 8192 + dc * 1024 + bs * 128: buf * 8192 + dc * 1024 + (bs + 1) * 128]
                        rhs = WKVP[:, dc * PW:dc * PW + ncols]
                        e.matmul(slot[:, 0:ncols], lhsT, rhs, start=(dc == 0), stop=False)
                    bias = BKV[:, rp * PW:rp * PW + ncols]
                    return e.matmul(slot[:, 0:ncols], ONESR[:], bias, start=False, stop=True)
                P.op("pe", proj, waits=w, inc=("sPJ", 1))

                w = [("sPJ", rb + 1), ("sCSL", 32 * (rp * NSLAB + si + 1))]
                if not (rp == 0 and si == 0):
                    w.append(("sKTD", 16 * cumktd(rp, si - 1)))
                co = buf * 1024 + bs * 64
                par = rp % 2

                def rope2(e, slot=slot, co=co, kslot=bs, hasq=hasq, blk=ib, par=par):
                    c_ap = CSB[:, co:co + 64]
                    s_ap = CSB[:, co + 512:co + 512 + 64]
                    xr = slot[:, 0:128:2]
                    xi = slot[:, 1:128:2]
                    e.tensor_mul(TRS[:, 0:64], xr, c_ap)
                    e.tensor_mul(TRS[:, 64:128], xi, s_ap)
                    e.tensor_mul(TRS[:, 128:192], xr, s_ap)
                    e.tensor_mul(TRS[:, 192:256], xi, c_ap)
                    kt = KTOK[:, kslot * 128:(kslot + 1) * 128]
                    e.tensor_sub(kt[:, 0:128:2], TRS[:, 0:64], TRS[:, 64:128])
                    e.tensor_add(kt[:, 1:128:2], TRS[:, 128:192], TRS[:, 192:256])
                    last = e.tensor_copy(VAU[:, par * NCH * 129 + blk * 129: par * NCH * 129 + (blk + 1) * 129],
                                         slot[:, 128:KVC])
                    if hasq:
                        qr = slot[:, 257:385:2]
                        qi = slot[:, 258:385:2]
                        e.tensor_mul(TRS[:, 0:64], qr, c_ap)
                        e.tensor_mul(TRS[:, 64:128], qi, s_ap)
                        e.tensor_mul(TRS[:, 128:192], qr, s_ap)
                        e.tensor_mul(TRS[:, 192:256], qi, c_ap)
                        qt = QTOK[:, kslot * 128:(kslot + 1) * 128]
                        e.tensor_sub(qt[:, 0:128:2], TRS[:, 0:64], TRS[:, 64:128])
                        last = e.tensor_add(qt[:, 1:128:2], TRS[:, 128:192], TRS[:, 192:256])
                    return last
                P.op("dve", rope2, waits=w, inc=("sRP", 1))
            return f

        def batch_unit(s):
            def f(s=s):
                par = rp % 2
                off = par * S + s * 1024

                def trd(e, off=off):
                    dst = KT[:, off:off + 1024].rearrange("p (j t) -> p j t", j=8)
                    return e.dma_start_transpose(dst, KTOK[:, 0:1024])
                P.op("sp", trd, waits=[("sRP", rb0 + s * 8 + 8)], inc=("sKTD", 16))
                if s <= 1:
                    offq = par * SL + s * 1024

                    def trq(e, offq=offq):
                        dst = QT[:, offq:offq + 1024].rearrange("p (j t) -> p j t", j=8)
                        return e.dma_start_transpose(dst, QTOK[:, 0:1024])
                    P.op("sp", trq, waits=[("sRP", rb0 + s * 8 + 8)], inc=("sKTD", 16))
            return f

        for si in range(NSLAB):
            units.append(slab_unit(si))
            if si >= 1:
                units.append(batch_unit(si - 1))
            for ib in range(si * 8, si * 8 + 8):
                units.append(block_unit(ib))
        units.append(batch_unit(NSLAB - 1))

        def done_unit():
            P.op("dve", lambda e: e.engine_nop(), waits=[("sKTD", 16 * KTD_PP * (rp + 1))], inc=("sRC", 1))
        units.append(done_unit)
        return units

    def _avu(p, qq, glist, g, u, PT, ptstride, PO, voff, sE, sAV):
        glen = glist[g]
        cbase = sum(glist[:g])
        buf = u % 4
        waits = [(sE, u + 1)]
        if g == 0 and qq > 0:
            waits.append(("sDVA", qq))
        par = p % 2

        def av(e, glen=glen, cbase=cbase, buf=buf, g=g, voff=voff, par=par):
            last = None
            for j in range(glen):
                c = cbase + j
                last = e.matmul(PO[0:65, 0:QW],
                                VAU[:, par * NCH * 129 + c * 129 + voff: par * NCH * 129 + c * 129 + voff + 65],
                                PT[:, buf * ptstride + j * 512: buf * ptstride + (j + 1) * 512],
                                start=(g == 0 and j == 0), stop=(g == len(glist) - 1 and j == glen - 1))
            return last
        P.op("pe", av, waits=waits, inc=(sAV, 1))

    pending_epi = []
    pending_a1 = []

    def emit_pair(p):
        par = p % 2
        units = rc_units(p + 1) if p + 1 < NP else []
        uidx = [0]
        pump_span = 3 * NGRB

        def pump_rc(s_):
            while uidx[0] < len(units) and uidx[0] * pump_span < (s_ + 1) * len(units):
                units[uidx[0]]()
                uidx[0] += 1

        for wq_i in range(NQW):
            ubA = (p * NQW + wq_i) * NGR
            ubB = (p * NQW + wq_i) * NGRB
            qq = p * NQW + wq_i
            qoff = wq_i * QW
            aidx = [0]
            avA = [0]
            avB = [0]

            def sc_step(k, g):
                uB = ubB + k
                cbaseB = sum(GRPB[:k])
                w = []
                if uB > 0:
                    w.append(("sEB", uB))
                if g is not None:
                    uA = ubA + g
                    glenA = GRP[g]
                    cbaseA = sum(GRP[:g])
                    if uA > 0:
                        w.append(("sEA", uA))
                if k == 0 and wq_i == 0:
                    w.append(("sRC", p + 1))

                def mk_a(j, qoff=qoff, par=par):
                    c = cbaseA + j
                    return lambda e, j=j, c=c, qoff=qoff, par=par: e.matmul(
                        PX[:, j * 512:(j + 1) * 512],
                        KT[0:64, par * S + c * 128: par * S + (c + 1) * 128],
                        QT[0:64, par * SL + qoff: par * SL + qoff + QW],
                        start=True, stop=True)

                def mk_b(j, qoff=qoff, par=par):
                    c = cbaseB + j
                    return lambda e, j=j, c=c, qoff=qoff, par=par: e.matmul(
                        PY[:, j * 512:(j + 1) * 512],
                        KT[64:128, par * S + c * 128: par * S + (c + 1) * 128],
                        QT[64:128, par * SL + qoff: par * SL + qoff + QW],
                        start=True, stop=True, tile_position=(64, 0))
                seq = []
                if g is None:
                    seq = [(mk_b(0), "B"), (mk_b(1), "B")]
                else:
                    a_mms = [(mk_a(j), "A") for j in range(glenA)]
                    b_mms = [(mk_b(0), "B"), (mk_b(1), "B")]
                    seq = []
                    for i in range(max(len(a_mms), len(b_mms))):
                        if i < len(a_mms):
                            seq.append(a_mms[i])
                        if i < len(b_mms):
                            seq.append(b_mms[i])
                lastA = max((i for i, s_ in enumerate(seq) if s_[1] == "A"), default=-1)
                lastB = max(i for i, s_ in enumerate(seq) if s_[1] == "B")

                def bundle(e, seq=seq, lastA=lastA, lastB=lastB):
                    last = None
                    for i, (fn, _) in enumerate(seq):
                        last = fn(e)
                        if i == lastA:
                            last.then_inc(sem["sSA"], 1)
                        if i == lastB:
                            last.then_inc(sem["sSB"], 1)
                    return last
                P.op("pe", bundle, waits=w, inc=None,
                     sim_incs=([("sSA", 1)] if g is not None else []) + [("sSB", 1)])
                wB = [("sSB", uB + 1)]
                if uB >= 4:
                    wB.append(("sAVB", uB - 3))
                P.op("act", lambda e, buf=uB % 4: e.activation(
                    PTB[:, buf * 1024:buf * 1024 + 1024], PY[:, 0:1024], AF.Exp,
                    bias=CB[:, C_SHIFT:C_SHIFT + 1], scale=CB[:, C_SCL:C_SCL + 1]),
                    waits=wB, inc=("sEB", 1))
                if g is not None:
                    fd = glenA * 512
                    wA = [("sSA", uA + 1)]
                    if uA >= 4:
                        wA.append(("sAVA", uA - 3))
                    P.op("act", lambda e, fd=fd, buf=uA % 4: e.activation(
                        PTA[:, buf * 1536:buf * 1536 + fd], PX[:, 0:fd], AF.Exp,
                        bias=CB[:, C_SHIFT:C_SHIFT + 1], scale=CB[:, C_SCL:C_SCL + 1]),
                        waits=wA, inc=("sEA", 1))

            for k in range(NGRB):
                g = None
                if aidx[0] * NGRB < (k + 1) * NGR:
                    g = aidx[0]
                    aidx[0] += 1
                sc_step(k, g)
                while aidx[0] * NGRB < (k + 1) * NGR:
                    sc_step_extra = aidx[0]
                    aidx[0] += 1
                    raise AssertionError("unexpected extra A group")
                if k == 3:
                    for f in pending_epi:
                        f()
                    pending_epi.clear()
                if k in (8, 14, 20, 26) and pending_a1:
                    pending_a1.pop(0)()
                if k >= 3:
                    while avB[0] <= k - 3:
                        _avu(p, qq, GRPB, avB[0], ubB + avB[0], PTB, 1024, POB, 64, "sEB", "sAVB")
                        avB[0] += 1
                    while (avA[0] + 3) * NGRB < (k + 1) * NGR:
                        _avu(p, qq, GRP, avA[0], ubA + avA[0], PTA, 1536, POA, 0, "sEA", "sAVA")
                        avA[0] += 1
                pump_rc(wq_i * NGRB + k)
            while avA[0] < NGR:
                _avu(p, qq, GRP, avA[0], ubA + avA[0], PTA, 1536, POA, 0, "sEA", "sAVA")
                avA[0] += 1
            while avB[0] < NGRB:
                _avu(p, qq, GRPB, avB[0], ubB + avB[0], PTB, 1024, POB, 64, "sEB", "sAVB")
                avB[0] += 1

            def epi(qq=qq, qoff=qoff, p=p, wq_i=wq_i):
                P.op("dve", lambda e: e.tensor_copy(OEA[0:65, :], POA[0:65, :]),
                     waits=[("sAVA", (qq + 1) * NGR)], inc=("sOEA", 1), flush=True)
                P.op("dve", lambda e: e.tensor_copy(OEB[0:65, :], POB[0:65, :]),
                     waits=[("sAVB", (qq + 1) * NGRB)], inc=("sOEB", 1), flush=True)

                def tra(e):
                    last = None
                    for j in range(4):
                        last = e.transpose(POA[:, j * 128:(j + 1) * 128], OEA[:, j * 128:(j + 1) * 128], EYEF[:])
                    return last
                P.op("pe", tra, waits=[("sOEA", qq + 1)], inc=("sTRA", 1))

                def trb_(e):
                    last = None
                    for j in range(4):
                        last = e.transpose(POB[:, j * 128:(j + 1) * 128], OEB[:, j * 128:(j + 1) * 128], EYEF[:])
                    return last
                P.op("pe", trb_, waits=[("sOEB", qq + 1)], inc=("sTRB", 1))

                w = [("sTRA", qq + 1), ("sTRB", qq + 1)]
                if qq > 0:
                    w.append(("sOSW", 16 * 8 * qq))

                def dv(e):
                    for j in range(4):
                        e.reciprocal(RD[:, j:j + 1], POA[:, j * 128 + 64:j * 128 + 65])
                    for j in range(4):
                        e.reciprocal(RD[:, 4 + j:5 + j], POB[:, j * 128:j * 128 + 1])
                    e.drain()
                    last = None
                    for j in range(4):
                        e.tensor_scalar(OTD[:, j * 64:(j + 1) * 64], POA[:, j * 128:j * 128 + 64],
                                        RD[:, j:j + 1], None, ALU.mult)
                        last = e.tensor_scalar(OTD2[:, j * 64:(j + 1) * 64], POB[:, j * 128 + 1:j * 128 + 65],
                                               RD[:, 4 + j:5 + j], None, ALU.mult)
                    return last
                P.op("dve", dv, waits=w, inc=("sDVA", 1), flush=True)
                for j in range(4):
                    dst = d_osc[qoff + j * 128: qoff + (j + 1) * 128, (2 * p) * 64:(2 * p + 1) * 64]
                    P.op("pool", lambda e, d=dst, j=j: e.dma_start(d, OTD[:, j * 64:(j + 1) * 64]),
                         waits=[("sDVA", qq + 1)], inc=("sOSW", 16))
                for j in range(4):
                    dst = d_osc[qoff + j * 128: qoff + (j + 1) * 128, (2 * p + 1) * 64:(2 * p + 2) * 64]
                    P.op("pool", lambda e, d=dst, j=j: e.dma_start(d, OTD2[:, j * 64:(j + 1) * 64]),
                         waits=[("sDVA", qq + 1)], inc=("sOSW", 16))
                if p == NP - 1:
                    for ti in range(wq_i * 4, wq_i * 4 + 4):
                        pending_a1.append(lambda ti=ti: (emit_a1(ti), emit_a2_front(ti),
                                                         ti >= 1 and emit_a2_back(ti - 1)))
            pending_epi.append(epi)
        pump_rc(10 ** 9)

    def emit_a2_front(i):
        buf = 2 + i % 2
        w = [("s3X1", 16 * (i + 1))]
        if i >= 2:
            w.append(("s3N", i - 1))
        P.op("sp", lambda e, i=i, b=buf: e.dma_start(X1[:, b * DM:(b + 1) * DM],
                                                     d_x1sc[i * 128:(i + 1) * 128, :]),
             waits=w, inc=("sOSR", 16))
        wap = [("sOSR", 16 * (2 * i + 2))]
        if i >= 2:
            wap.append(("s3TD", 16 * (i - 1)))
        P.op("dve", lambda e, i=i, b=buf, b2=i % 2: e.tensor_scalar(X1N[:, b2 * DM:(b2 + 1) * DM],
                                                                    X1[:, b * DM:(b + 1) * DM],
                                                                    MVS3[:, i:i + 1], RSS3[:, i:i + 1],
                                                                    ALU.subtract, ALU.mult),
             waits=wap, inc=None, flush=False)
        P.op("dve", lambda e: e.drain(), inc=("s3N", 1))
        wtd = [("s3N", i + 1)]
        if i >= 4:
            wtd.append(("s3S", i - 3))

        def t3d(e, b2=i % 2, b4=i % 4):
            dst = X1NT[:, b4 * DM:(b4 + 1) * DM].rearrange("p (j t) -> p j t", j=8)
            return e.dma_start_transpose(dst, X1N[:, b2 * DM:(b2 + 1) * DM])
        P.op("sp", t3d, waits=wtd, inc=("s3TD", 16))

    def emit_a2_back(i):
        def smm(e, b4=i % 4, sl=i % 2):
            ps = SPARE[0:64, sl * 128:(sl + 1) * 128]
            for dc in range(8):
                e.matmul(ps, FFNW[:, dc * 64:(dc + 1) * 64],
                         X1NT[:, b4 * DM + dc * 128:b4 * DM + (dc + 1) * 128],
                         start=(dc == 0), stop=False)
            return e.matmul(ps, BIN[:], ONESR[:, 0:128], start=False, stop=True)
        wsm = [("s3TD", 16 * (i + 1))]
        if i >= 2:
            wsm.append(("s3S2", i - 1))
        P.op("pe", smm, waits=wsm, inc=("s3S", 1))
        P.op("dve", lambda e, i=i, sl=i % 2: e.tensor_copy(SALL[:, i * 128:(i + 1) * 128],
                                                           SPARE[0:64, sl * 128:(sl + 1) * 128]),
             waits=[("s3S", i + 1)], inc=("s3S2", 1), flush=True)

    def emit_mid(half):
        def mid(e):
            for h in range(16):
                e.tensor_scalar(TM3[:], SALL[:, half * 1024:(half + 1) * 1024],
                                FFS[0:64, h:h + 1], FFS[0:64, 16 + h:17 + h],
                                ALU.mult, ALU.add)
                e.tensor_scalar(TM3[:], TM3[:], CB[0:64, C_ZERO:C_ZERO + 1], None, ALU.max)
                if h == 0:
                    e.tensor_scalar(G3[:], TM3[:], FFS[0:64, 32:33], FFS[0:64, 48:49],
                                    ALU.mult, ALU.add)
                else:
                    e.scalar_tensor_tensor(G3[:], TM3[:], FFS[0:64, 32 + h:33 + h], G3[:],
                                           ALU.mult, ALU.add)
                e.drain()
            e.memset(GB3[64:65, :], 1.0)
            return e.tensor_copy(GB3[0:64, :], G3[:])
        wmid = [("s3S2", half * 8 + 8)]
        if half == 0:
            wmid += [("sEA", NP * NQW * NGR), ("sEB", NP * NQW * NGRB)]
        else:
            wmid.append(("s3O", 8))
        P.op("dve", mid, waits=wmid, inc=("s3M", 1), flush=True)

    def emit_c(i):
        buf = i % 2
        w = [("s3M", i // 8 + 1)]
        if i < 2:
            w.append(("s3A1", 16))
        if i >= 2:
            w.append(("s3F", i - 1))
        P.op("sp", lambda e, i=i, b=buf: e.dma_start(XQT[:, b * DM:(b + 1) * DM],
                                                     d_x1sc[i * 128:(i + 1) * 128, :]),
             waits=w, inc=("sXQL", 16))
        PP = PX[:, 0:1024] if i % 2 == 0 else PY[:, 0:1024]

        def omm2(e, i=i, PP=PP):
            last = None
            for h2 in range(2):
                last = e.matmul(PP[:, h2 * 512:(h2 + 1) * 512],
                                GB3[:, (i % 8) * 128:(i % 8 + 1) * 128],
                                WOUT[:, h2 * 512:(h2 + 1) * 512],
                                start=True, stop=True)
            return last
        wo = [("s3M", i // 8 + 1)]
        if i >= 2:
            wo.append(("s3F", i - 1))
        P.op("pe", omm2, waits=wo, inc=("s3O", 1))
        P.op("dve", lambda e, PP=PP: e.bn_stats(ST3[:, 0:6], PP[:, 0:512]), waits=[("s3O", i + 1)])
        P.op("dve", lambda e, PP=PP: e.bn_stats(ST3[:, 6:12], PP[:, 512:1024]))
        P.op("dve", lambda e: (e.drain(), e.bn_aggr(MV3[:], ST3[:]))[1], inc=("s3ST", 1), flush=True)
        P.op("act", lambda e: e.activation(T1[:], MV3[:, 1:2], AF.Ln,
                                           bias=CB[:, C_EPS:C_EPS + 1], scale=CB[:, C_ONE:C_ONE + 1]),
             waits=[("s3ST", 32 + i + 1)])
        P.op("act", lambda e: e.activation(RS[:, 0:1], T1[:], AF.Exp,
                                           bias=CB[:, C_ZERO:C_ZERO + 1], scale=CB[:, C_NHALF:C_NHALF + 1]),
             inc=("s3RS", 1), flush=True)
        P.op("dve", lambda e, PP=PP: e.tensor_scalar(TMP3[:], PP[:, 0:1024], MV3[:, 0:1], RS[:, 0:1],
                                                     ALU.subtract, ALU.mult),
             waits=[("s3RS", 32 + i + 1)])
        wyt = [("sYW", 16 * i)] if i >= 1 else []
        P.op("dve", lambda e, b=buf: e.tensor_add(YT[:], TMP3[:], XQT[:, b * DM:(b + 1) * DM]),
             waits=wyt + [("sXQL", 16 * (17 + i))], inc=("s3F", 1), flush=True)
        P.op("pool", lambda e, i=i: e.dma_start(d_y[i * 128:(i + 1) * 128, :], YT[:]),
             waits=[("s3F", i + 1)], inc=("sYW", 16))

    if PHASES == "all":
        for u in rc_units(0):
            u()
        for p in range(NP):
            emit_pair(p)
        for f in pending_epi:
            f()
        pending_epi.clear()
        for f in pending_a1:
            f()
        pending_a1.clear()
        emit_a2_back(15)

    if PHASES == "all":
        emit_mid(0)
        for j in range(8):
            emit_c(j)
        emit_mid(1)
        for j in range(8, 16):
            emit_c(j)

    FINAL_WAIT = ("sYW", 16 * 16) if PHASES == "all" else ("sXWR", XWR_ALL)
    P.op("sp", lambda e: e.nop(), waits=[FINAL_WAIT])
    if DEBUG_DUMPS:
        d_dosc = nc.dram_tensor("dbg_osc", [SL, DM], F32, kind="ExternalOutput").ap()
        d_dxlt = nc.dram_tensor("dbg_xlt", [8, 128, 1024], BF16, kind="ExternalOutput").ap()
        d_dxlq = nc.dram_tensor("dbg_xltq", [8, 128, 1024], BF16, kind="ExternalOutput").ap()
        d_dkt = nc.dram_tensor("dbg_kt", [128, S], BF16, kind="ExternalOutput").ap()
        d_dqt = nc.dram_tensor("dbg_qt", [128, SL], BF16, kind="ExternalOutput").ap()
        d_dva = nc.dram_tensor("dbg_vau", [128, 10 * 129], BF16, kind="ExternalOutput").ap()
        P.op("sp", lambda e: e.dma_start(d_dosc, d_osc[:]), waits=[FINAL_WAIT], inc=("sDBG", 16))
        P.op("sp", lambda e: e.dma_start(d_dxlt, d_xlt[:, :, 0:1024]), inc=("sDBG", 16))
        P.op("sp", lambda e: e.dma_start(d_dxlq, d_xlt[:, :, 0:1024]), inc=("sDBG", 16))
        P.op("sp", lambda e: e.dma_start(d_dkt, KT[:, S:2 * S]), inc=("sDBG", 16))
        P.op("sp", lambda e: e.dma_start(d_dqt, QT[:, SL:2 * SL]), inc=("sDBG", 16))
        P.op("sp", lambda e: e.dma_start(d_dva, VAU[:, NCH * 129:NCH * 129 + 10 * 129]), inc=("sDBG", 16))
        P.op("sp", lambda e: e.nop(), waits=[("sDBG", 96)])

    build_nc.last_prog = P
    with nc.Block() as block:
        def body_for(name):
            def body(eng):
                for waits, fn, inc, flush in P.items[name]:
                    for (s, v) in waits:
                        if v > 0:
                            eng.wait_ge(sem[s], v)
                    instr = fn(eng)
                    if inc is not None:
                        if flush:
                            eng.drain().then_inc(sem[inc[0]], inc[1])
                        else:
                            instr.then_inc(sem[inc[0]], inc[1])
            return body
        block.sync(body_for("sp"))
        block.tensor(body_for("pe"))
        block.scalar(body_for("act"))
        block.vector(body_for("dve"))
        block.gpsimd(body_for("pool"))

    return nc


def _emit_av(P, p, wq_i, g, u, PT, VAU, PO, voff, sE, sAV, sDV):
    glen = GRP[g]
    cbase = sum(GRP[:g])
    buf = u % 2
    qq = p * NQW + wq_i
    waits = [(sE, u + 1)]
    if g == 0 and qq > 0:
        waits.append((sDV, qq))
    def av(e, glen=glen, cbase=cbase, buf=buf, g=g, voff=voff):
        last = None
        for j in range(glen):
            c = cbase + j
            last = e.matmul(PO[0:65, 0:QW],
                            VAU[:, c * 129 + voff: c * 129 + voff + 65],
                            PT[:, buf * 1536 + j * 512: buf * 1536 + (j + 1) * 512],
                            start=(g == 0 and j == 0), stop=(g == NGR - 1 and j == glen - 1))
        return last
    P.op("pe", av, waits=waits, inc=(sAV, 1))


_NC_CACHE = {}


def _get_nc():
    if "nc" not in _NC_CACHE:
        _NC_CACHE["nc"] = build_nc()
    return _NC_CACHE["nc"]


def kernel(L0_attn_x, L0_attn_wq_w, L0_attn_wq_b, L0_attn_wk_w, L0_attn_wk_b,
           L0_attn_wv_w, L0_attn_wv_b, L0_ffn_Win, L0_ffn_bin_, L0_ffn_W1,
           L0_ffn_b1, L0_ffn_W2, L0_ffn_b2, L0_ffn_Wout, L0_ffn_bout):
    x = np.asarray(L0_attn_x, dtype=np.float32)
    wq = np.asarray(L0_attn_wq_w, np.float32)
    wk = np.asarray(L0_attn_wk_w, np.float32)
    wv = np.asarray(L0_attn_wv_w, np.float32)
    bq = np.asarray(L0_attn_wq_b, np.float32)
    bk = np.asarray(L0_attn_wk_b, np.float32)
    bv = np.asarray(L0_attn_wv_b, np.float32)

    wqT = np.ascontiguousarray(wq.T)
    wkT = np.ascontiguousarray(wk.T)
    wvT = np.ascontiguousarray(wv.T)

    wkv_h = np.zeros((8, 128, NP * PW), np.float32)
    bkv_h = np.zeros((1, NP * PW), np.float32)
    for p in range(NP):
        hA, hB = 2 * p, 2 * p + 1
        base = p * PW
        colsK = slice(base, base + 128)
        colsVA = slice(base + 128, base + 192)
        colsVB = slice(base + 193, base + 257)
        colsQ = slice(base + 257, base + 385)
        for dc in range(8):
            din = slice(dc * 128, (dc + 1) * 128)
            wkv_h[dc, :, colsK] = wkT[din, hA * 64:hB * 64 + 64]
            wkv_h[dc, :, colsVA] = wvT[din, hA * 64:(hA + 1) * 64]
            wkv_h[dc, :, colsVB] = wvT[din, hB * 64:(hB + 1) * 64]
            wkv_h[dc, :, colsQ] = wqT[din, hA * 64:hB * 64 + 64]
        bkv_h[0, colsK] = bk[hA * 64:hB * 64 + 64]
        bkv_h[0, colsVA] = bv[hA * 64:(hA + 1) * 64]
        bkv_h[0, base + 192] = 1.0
        bkv_h[0, colsVB] = bv[hB * 64:(hB + 1) * 64]
        bkv_h[0, colsQ] = bq[hA * 64:hB * 64 + 64]

    theta = (10000.0 ** (-(np.arange(32, dtype=np.float64) / 32.0))).astype(np.float64)
    pos = np.arange(S, dtype=np.float64)
    ang = np.outer(pos, theta)
    cosf = np.cos(ang).astype(np.float32)
    sinf = np.sin(ang).astype(np.float32)

    def mk_cs(tokens):
        nb = len(tokens) // 128
        c = np.zeros((128, nb * 64), np.float32)
        s = np.zeros((128, nb * 64), np.float32)
        for b in range(nb):
            tt = tokens[b * 128:(b + 1) * 128]
            cc = cosf[tt]
            ss = sinf[tt]
            c[:, b * 64:b * 64 + 32] = cc
            c[:, b * 64 + 32:b * 64 + 64] = cc
            s[:, b * 64:b * 64 + 32] = ss
            s[:, b * 64 + 32:b * 64 + 64] = ss
        return np.concatenate([c, s], axis=1)

    csk_h = mk_cs(np.arange(S))

    cb_h = np.zeros((128, 8), np.float32)
    cb_h[:, C_SHIFT] = -SHIFT
    cb_h[:, C_SCL] = 1.0 / np.sqrt(HD)
    cb_h[:, C_ONE] = 1.0
    cb_h[:, C_ZERO] = 0.0
    cb_h[:, C_NHALF] = -0.5
    cb_h[:, C_EPS] = EPS

    eyef_h = np.eye(128).astype(np.float32)

    win_h = np.asarray(L0_ffn_Win, np.float32).reshape(8, 128, 64)
    wout_h = np.vstack([np.asarray(L0_ffn_Wout, np.float32),
                        np.asarray(L0_ffn_bout, np.float32)[None, :]])
    ffs_h = np.zeros((64, 50), np.float32)
    ffs_h[:, 0:16] = np.asarray(L0_ffn_W1, np.float32)
    ffs_h[:, 16:32] = np.asarray(L0_ffn_b1, np.float32)
    ffs_h[:, 32:48] = np.asarray(L0_ffn_W2, np.float32)
    ffs_h[:, 48] = np.asarray(L0_ffn_b2, np.float32)
    fbr_h = np.zeros((1, DM + 64), np.float32)
    fbr_h[0, 0:64] = np.asarray(L0_ffn_bin_, np.float32)
    fbr_h[0, 64:] = np.asarray(L0_ffn_bout, np.float32)

    shared = {
        "wkv": wkv_h, "bkv": bkv_h,
        "cbv": cb_h, "eyef": eyef_h,
        "ffw_in": win_h, "ffw_out": wout_h, "ffs": ffs_h, "fbr": fbr_h,
    }

    in_maps = []
    for c in range(NC):
        b = c // 4
        sl = (c % 4) * SL
        rot = np.r_[np.arange(sl, sl + SL), np.arange(0, sl), np.arange(sl + SL, S)]
        m = dict(shared)
        m["x"] = np.ascontiguousarray(x[b][rot])
        m["csk"] = mk_cs(rot)
        in_maps.append(m)

    nc = _get_nc()
    res = run_bass_kernel_spmd(nc, in_maps, core_ids=list(range(NC)))
    global LAST_RES
    LAST_RES = res
    out = np.empty((B, S, DM), np.float32)
    for c in range(NC):
        b = c // 4
        sl = (c % 4) * SL
        out[b, sl:sl + SL] = np.asarray(res.results[c]["y"], np.float32)
    return out
